# version 27
# baseline (speedup 1.0000x reference)
import numpy as np
import ml_dtypes
from contextlib import ExitStack
import concourse.bass as bass
import concourse.mybir as mybir
from concourse.bass_utils import run_bass_kernel_spmd

F32 = mybir.dt.float32
BF16 = mybir.dt.bfloat16
I32 = mybir.dt.int32
AF = mybir.ActivationFunctionType
ALU = mybir.AluOpType
AX = mybir.AxisListType

D = 2048
HID = 5632
S = 4096
OWN = 2048
NH = 32
DC = D // 128
HCN = HID // 128
HTN = HID // 512
NTT = OWN // 512
NKC = S // 128
EPS_LN = 1e-5
EPS_RMS = 1e-8

ENGINES = ("tensor", "vector", "scalar", "sync", "gpsimd")
DEBUG = False


class Sem:
    def __init__(self, h):
        self.h = h
        self.n = 0


class Prog:

    def __init__(self):
        self.ops = []

    def emit(self, engine, fn, waits=(), incs=()):
        rec = []
        for (s, amt) in incs:
            s.n += amt
            rec.append((s.h, amt))
        ws = [w for w in waits if w is not None]
        self.ops.append((engine, fn, ws, rec))

    def dma(self, engine, out_ap, in_ap, sem, transpose=False, waits=()):
        def fn(e, o=out_ap, i=in_ap, t=transpose):
            if t:
                return e.dma_start(out=o, in_=i, transpose=True)
            return e.dma_start(out=o, in_=i)
        self.emit(engine, fn, waits=waits, incs=[(sem, 16)])
        return (sem.h, sem.n)

    def barrier(self, data_sems, prog_sems):
        targets = [(s.h, s.n) for s in data_sems if s.n > 0]
        for e in ENGINES:
            ps = prog_sems[e]
            self.emit(e, lambda eng: eng.nop(), incs=[(ps, 1)])
        ptargets = [(prog_sems[e].h, prog_sems[e].n) for e in ENGINES]
        for e in ENGINES:
            self.emit(e, lambda eng: eng.nop(), waits=ptargets + targets)


def build_nc():
    nc = bass.Bass("TRN2", target_bir_lowering=False)

    x_d = nc.dram_tensor("x", [S, D], F32, kind="ExternalInput").ap()
    wqT_d = nc.dram_tensor("wqT", [D, D], F32, kind="ExternalInput").ap()
    wkT_d = nc.dram_tensor("wkT", [D, 128], F32, kind="ExternalInput").ap()
    wvT_d = nc.dram_tensor("wvT", [D, 128], F32, kind="ExternalInput").ap()
    woT_d = nc.dram_tensor("woT", [D, D], F32, kind="ExternalInput").ap()
    gT_d = nc.dram_tensor("gT", [D, HID], I32, kind="ExternalInput").ap()
    uT_d = nc.dram_tensor("uT", [D, HID], I32, kind="ExternalInput").ap()
    wd_d = nc.dram_tensor("wd", [HID, D], I32, kind="ExternalInput").ap()
    sg_d = nc.dram_tensor("sg", [1, HID], F32, kind="ExternalInput").ap()
    su_d = nc.dram_tensor("su", [1, HID], F32, kind="ExternalInput").ap()
    sd_d = nc.dram_tensor("sdr", [1, D], F32, kind="ExternalInput").ap()
    ns_d = nc.dram_tensor("nsr", [128, DC], F32, kind="ExternalInput").ap()
    ss_d = nc.dram_tensor("ssr", [128, DC], F32, kind="ExternalInput").ap()
    obd_d = nc.dram_tensor("ones_bd", [128, 2], BF16, kind="ExternalInput").ap()
    out_d = nc.dram_tensor("out", [OWN, D], F32, kind="ExternalOutput").ap()

    KI = "ExternalOutput" if DEBUG else "Internal"
    xn_s = nc.dram_tensor("xn_s", [OWN, D], BF16, kind=KI).ap()
    h_s = nc.dram_tensor("h_s", [S, D], BF16, kind=KI).ap()
    gu_s = nc.dram_tensor("gu_s", [HID, OWN], BF16, kind=KI).ap()
    wd_s = nc.dram_tensor("wd_s", [HID, D], BF16, kind=KI).ap()
    q_s = nc.dram_tensor("q_s", [D, OWN], BF16, kind=KI).ap()
    m_s = nc.dram_tensor("m_s", [NH, OWN], BF16, kind=KI).ap()
    r_s = nc.dram_tensor("r_s", [NH, OWN], F32, kind=KI).ap()
    c_s = nc.dram_tensor("c_s", [1, D], F32, kind=KI).ap()
    mk2_s = nc.dram_tensor("mk2_s", [2, 1], F32, kind=KI).ap()
    dbg_s = nc.dram_tensor("dbg_s", [128, 64], F32, kind=KI).ap()

    ES = ExitStack()
    sbt = lambda name, shape, dt: ES.enter_context(nc.sbuf_tensor(name, shape, dt))

    U1 = sbt("U1", [128, DC * D], BF16)
    U2b = sbt("U2b", [128, 32768], BF16)
    U2a = sbt("U2a", [128, 8192], F32)
    Kp = [sbt(f"Kp{g}", [128, S], BF16) for g in range(2)]
    Vp = [sbt(f"Vp{g}", [128, NKC * 65], BF16) for g in range(2)]
    stats_sb = sbt("stats_sb", [128, 32], F32)
    stv = sbt("stv_tmp", [128, 48], F32)
    sts = sbt("sts_tmp", [128, 16], F32)
    vpad = sbt("vpad", [128, 32], F32)
    spad = sbt("spad", [128, 256], F32)
    junkv = sbt("junkv", [128, 512], F32)
    cst = sbt("cst", [128, 8], F32)
    ones_bd = sbt("ones_bdb", [128, 2], BF16)
    mk2bc = sbt("mk2bc", [2, 2], F32)
    mk2acc = sbt("mk2acc", [1, 4], F32)
    jx = sbt("jx", [128, 8], F32)

    x_ring = [U2a[:, i * 2048:(i + 1) * 2048] for i in range(2)]
    x3 = U2a[:, 4096:6144]
    xn_st = [U2b[:, 24576 + i * 2048:24576 + (i + 1) * 2048] for i in range(2)]
    h_st = [U2b[:, 28672 + i * 2048:28672 + (i + 1) * 2048] for i in range(2)]
    sqf = sbt("sqf", [128, 2048], F32)
    stB = sbt("stB", [128, 320], F32)
    cT = sbt("cT", [128, DC], F32)
    hT = [U2b[:, i * 8192:(i + 1) * 8192].rearrange("p (d n) -> p d n", d=DC)
          for i in range(2)]
    k2st = U2b[:, 16384:20480]
    wkv = U2b[:, 20480:24576].rearrange("p (d n) -> p d n", d=DC)
    qt_st = [U2b[:, 24576 + i * 512:24576 + (i + 1) * 512] for i in range(2)]
    q2_st = [U2b[:, 25600 + i * 512:25600 + (i + 1) * 512] for i in range(2)]
    wdraw = [U2b[:, 16384:18432], U2b[:, 26624:28672]]
    sd_bc = U2a[:, 6144:8192]
    mneg_st = U2a[:, 0:512]
    mneg2 = U2a[:, 512:1024]
    wg_ring = [U2b[:, i * 8192:(i + 1) * 8192].rearrange("p (d n) -> p d n", d=DC)
               for i in range(2)]
    wu_ring = [U2b[:, 16384 + i * 8192:16384 + (i + 1) * 8192].rearrange(
        "p (d n) -> p d n", d=DC) for i in range(2)]
    sbc_g = [U2a[:, i * 512:(i + 1) * 512] for i in range(2)]
    sbc_u = [U2a[:, 1024 + i * 512:1024 + (i + 1) * 512] for i in range(2)]
    silu_st = [U2a[:, 2048 + i * 512:2048 + (i + 1) * 512] for i in range(4)]
    gu_sb = sbt("gu_sb", [128, 2048], BF16)
    gu_st = [gu_sb[:, i * 512:(i + 1) * 512] for i in range(4)]
    qp_ring = [U2b[:, i * 512:(i + 1) * 512] for i in range(4)]
    probs = [U2b[:, 2048 + i * 1024:2048 + (i + 1) * 1024] for i in range(4)]
    aot_sb = U2b[:, 18432:26624]
    r_st = [U2a[:, i * 512:(i + 1) * 512] for i in range(2)]
    rb_st = [U2a[:, 1024 + i * 512:1024 + (i + 1) * 512] for i in range(2)]
    guT = [U2b[:, 8192 + i * 512:8192 + (i + 1) * 512] for i in range(4)]
    wdT = [U2b[:, 12288 + i * 1024:12288 + (i + 1) * 1024] for i in range(4)]
    xre = [U2a[:, 2048 + i * 1024:2048 + (i + 1) * 1024] for i in range(2)]
    outst = [U2a[:, 4096 + i * 512:4096 + (i + 1) * 512] for i in range(4)]

    pgd = ES.enter_context(nc.psum_tensor("pgd", [128, 4096], F32))
    pg = [pgd[:, i * 1024:(i + 1) * 1024] for i in range(4)]

    names = ["s_x", "s_c", "s_w", "s_t", "s_wr", "s_q", "s_m", "s_r", "s_ao",
             "s_gu", "s_wd", "s_st", "s_mm", "s_ve", "s_se", "s_bc",
             "p_t", "p_v", "p_s", "p_y", "p_g"]
    SM = {n: Sem(ES.enter_context(nc.semaphore(n))) for n in names}
    prog_sems = {"tensor": SM["p_t"], "vector": SM["p_v"], "scalar": SM["p_s"],
                 "sync": SM["p_y"], "gpsimd": SM["p_g"]}
    data_sems = [SM[n] for n in names if not n.startswith("p_")]
    s_x, s_c, s_w, s_t, s_wr = (SM[n] for n in ["s_x", "s_c", "s_w", "s_t", "s_wr"])
    s_q, s_m, s_r, s_ao = (SM[n] for n in ["s_q", "s_m", "s_r", "s_ao"])
    s_gu, s_wd, s_st = (SM[n] for n in ["s_gu", "s_wd", "s_st"])
    s_mm, s_ve, s_se, s_bc = (SM[n] for n in ["s_mm", "s_ve", "s_se", "s_bc"])

    P = Prog()
    wv = lambda s: (s.h, s.n)
    bar = lambda: P.barrier(data_sems, prog_sems)

    nc.const_aps.register(F32, EPS_RMS, cst[:, 0:1])
    P.emit("vector", lambda e: e.memset(cst[:, 0:1], EPS_RMS), incs=[(s_ve, 1)])
    P.emit("vector", lambda e: e.memset(junkv[:, :], 0.0), incs=[(s_ve, 1)])
    P.emit("vector", lambda e: e.memset(vpad[:, :], 0.0), incs=[(s_ve, 1)])
    P.emit("vector", lambda e: e.memset(stv[:, :], 1.0), incs=[(s_ve, 1)])
    def swarm(e):
        e.memzero(spad[:, :])
        e.activation(spad[:, 0:64], spad[:, 64:128], AF.Square)
        e.activation(spad[:, 0:64], spad[:, 64:128], AF.Exp)
        return e.activation(spad[:, 0:64], spad[:, 0:64], AF.Ln)
    P.emit("scalar", swarm, incs=[(s_se, 1)])
    flushv = lambda e: e.tensor_copy(junkv[:, :], junkv[:, :])
    flushn = lambda e: e.tensor_copy(vpad[:, :], vpad[:, :])
    flushs = lambda e: e.activation(spad[:, :], spad[:, :], AF.Square)
    P.dma("sync", cT[:, :], ns_d[:, :], s_c)
    P.dma("sync", stB[:, 0:DC], ss_d[:, :], s_c)
    P.dma("sync", ones_bd[:, :], obd_d[:, :], s_c)
    def cmul(e):
        e.tensor_mul(cT[:, :], cT[:, :], stB[:, 0:DC])
        return flushv(e)
    P.emit("vector", cmul, waits=[wv(s_c)], incs=[(s_ve, 1)])
    cT_ready = wv(s_ve)
    c_spilled = cT_ready
    for dc in range(DC):
        P.dma("gpsimd", wkv[:, dc, 0:128], wkT_d[dc * 128:(dc + 1) * 128, :], s_w)
        P.dma("gpsimd", wkv[:, dc, 128:256], wvT_d[dc * 128:(dc + 1) * 128, :], s_w)
    def wkvscale(e):
        for dc in range(DC):
            e.tensor_scalar_mul(wkv[:, dc, :], wkv[:, dc, :], cT[:, dc:dc + 1])
        return flushv(e)
    P.emit("vector", wkvscale, waits=[wv(s_w), cT_ready], incs=[(s_ve, 1)])
    wkv_ready = wv(s_ve)

    invD = 1.0 / D
    SUM = stB[:, 0:32]
    SSQ = stB[:, 32:64]
    MEAN = stB[:, 64:96]
    VAR = stB[:, 96:128]
    VPE = stB[:, 128:160]
    A_ = stB[:, 160:192]
    NRM = stB[:, 192:224]
    R_ = stB[:, 224:256]
    AR_ = stB[:, 256:288]
    MA_ = stB[:, 288:320]
    sqv2 = [sqf, x3]
    l1_free = [None, None]
    sq_free = [None, None]
    for i in range(32):
        slot = i % 2
        xr = x_ring[slot]
        P.dma("sync", xr, x_d[i * 128:(i + 1) * 128, :], s_x,
              waits=[l1_free[slot]])
        xw = wv(s_x)
        P.emit("vector", lambda e, xr=xr, i=i: e.reduce_sum(
            SUM[:, i:i + 1], xr, axis=AX.X), waits=[xw], incs=[(s_ve, 1)])
        P.emit("scalar", lambda e, xr=xr, slot=slot: e.activation(
            sqv2[slot][:, :], xr, AF.Square),
            waits=[xw, sq_free[slot], c_spilled if i == 1 else None],
            incs=[(s_se, 1)])
        P.emit("vector", lambda e, i=i, slot=slot: e.reduce_sum(
            SSQ[:, i:i + 1], sqv2[slot][:, :], axis=AX.X),
            waits=[wv(s_se)], incs=[(s_ve, 1)])
        sq_free[slot] = wv(s_ve)
        l1_free[slot] = wv(s_ve)
    sq2_last = sq_free[1]
    def bemit(fn, waits=None):
        def g(e, fn=fn):
            fn(e)
            return flushv(e)
        P.emit("vector", g, waits=[wv(s_ve)] if waits is None else waits,
               incs=[(s_ve, 1)])
        return wv(s_ve)
    bemit(lambda e: e.nop())
    bemit(lambda e: (e.tensor_scalar_mul(MEAN, SUM, invD),
                     e.tensor_scalar_mul(VAR, SSQ, invD)))
    bemit(lambda e: e.tensor_mul(VPE, MEAN, MEAN))
    bemit(lambda e: e.tensor_sub(VAR, VAR, VPE))
    bemit(lambda e: e.tensor_scalar_add(VPE, VAR, EPS_LN))
    def sq_batch2(e):
        e.activation(NRM, VPE, AF.Sqrt)
        e.activation(R_, VAR, AF.Sqrt)
        return flushs(e)
    P.emit("scalar", sq_batch2, waits=[wv(s_ve)], incs=[(s_se, 1)])
    bemit(lambda e: e.reciprocal(A_, NRM), waits=[wv(s_se)])
    bemit(lambda e: e.tensor_mul(NRM, R_, A_))
    bemit(lambda e: e.tensor_scalar_add(NRM, NRM, EPS_RMS))
    bemit(lambda e: e.reciprocal(R_, NRM))
    bemit(lambda e: e.tensor_mul(AR_, R_, A_))
    bemit(lambda e: (e.tensor_mul(MA_, MEAN, A_),
                     e.tensor_scalar_add(stats_sb[:, 0:32:2], A_[:, 0:16],
                                         1.0)))
    bdone = bemit(lambda e: e.tensor_scalar_mul(stats_sb[:, 1:32:2],
                                                MA_[:, 0:16], -1.0))
    h_ev = []
    xr3 = [x_ring[0], x_ring[1], x3]
    l2_free = [l1_free[0], l1_free[1], sq2_last]
    a1_st_free = [None, None]
    for i in range(32):
        own = i < 16
        slot = i % 3
        st = i % 2
        xr = xr3[slot]
        P.dma("gpsimd", xr, x_d[i * 128:(i + 1) * 128, :], s_x,
              waits=[l2_free[slot]])
        xw = wv(s_x)
        def vap(e, i=i, xr=xr, st=st, own=own):
            if own:
                e.tensor_scalar(xn_st[st], xr, MEAN[:, i:i + 1], A_[:, i:i + 1],
                                op0=ALU.subtract, op1=ALU.mult)
            e.tensor_scalar(h_st[st], xr, MEAN[:, i:i + 1], AR_[:, i:i + 1],
                            op0=ALU.subtract, op1=ALU.mult)
            return flushv(e)
        P.emit("vector", vap,
               waits=[xw, a1_st_free[st], bdone if i == 0 else None],
               incs=[(s_ve, 1)])
        vv = wv(s_ve)
        l2_free[slot] = vv
        if own:
            P.dma("sync", xn_s[i * 128:(i + 1) * 128, :], xn_st[st], s_wr,
                  waits=[vv])
        last = P.dma("sync", h_s[i * 128:(i + 1) * 128, :], h_st[st], s_wr,
                     waits=[vv])
        a1_st_free[st] = last
        h_ev.append(last)
    bar()

    P.emit("vector", lambda e: e.memset(Kp[0][64:66, :], 0.0), incs=[(s_ve, 1)])
    P.emit("vector", lambda e: e.memset(Kp[1][64:66, :], 0.0), incs=[(s_ve, 1)])
    P.emit("vector", lambda e: e.memset(Kp[0][64:65, :], 1.0), incs=[(s_ve, 1)])
    P.emit("vector", lambda e: e.memset(Kp[1][64:65, :], 1.0), incs=[(s_ve, 1)])
    P.emit("vector", lambda e: e.memset(Vp[0][:, 64::65], 1.0), incs=[(s_ve, 1)])
    def vmemset4(e):
        e.memset(Vp[1][:, 64::65], 1.0)
        return flushv(e)
    P.emit("vector", vmemset4, incs=[(s_ve, 1)])

    hT_free = [None, None]
    kv_psum_free = None
    v_psum_free = [None, None]
    for tt in range(8):
        slot = tt % 2
        for dc in range(DC):
            P.dma("scalar", hT[slot][:, dc, :],
                  h_s[tt * 512:(tt + 1) * 512, dc * 128:(dc + 1) * 128],
                  s_t, transpose=True,
                  waits=[hT_free[slot] if dc == 0 else None,
                         h_ev[4 * tt + 3] if dc == 0 else None])
        hT_w = wv(s_t)

        def kmm(e, slot=slot):
            last = None
            for g in range(2):
                for dc in range(DC):
                    last = e.matmul(pg[0][0:64, g * 512:(g + 1) * 512],
                                    wkv[:, dc, g * 64:g * 64 + 64],
                                    hT[slot][:, dc, :],
                                    start=(dc == 0), stop=(dc == DC - 1))
            return last
        P.emit("tensor", kmm, waits=[hT_w, wkv_ready, kv_psum_free],
               incs=[(s_mm, 1)])
        kdone = wv(s_mm)
        def kevac(e, tt=tt):
            e.tensor_copy(Kp[0][0:64, tt * 512:(tt + 1) * 512], pg[0][0:64, 0:512])
            e.tensor_copy(Kp[1][0:64, tt * 512:(tt + 1) * 512],
                          pg[0][0:64, 512:1024])
            return flushv(e)
        P.emit("vector", kevac, waits=[kdone], incs=[(s_ve, 1)])
        kv_psum_free = wv(s_ve)
        for sub in range(4):
            half = sub % 2
            def vmm(e, slot=slot, sub=sub, half=half):
                last = None
                for dc in range(DC):
                    last = e.matmul(pg[1][:, half * 512:half * 512 + 128],
                                    hT[slot][:, dc, sub * 128:(sub + 1) * 128],
                                    wkv[:, dc, 128:256],
                                    start=(dc == 0), stop=(dc == DC - 1))
                return last
            P.emit("tensor", vmm, waits=[v_psum_free[half]], incs=[(s_mm, 1)])
            vdone = wv(s_mm)
            chunk = tt * 4 + sub
            def vevac(e, chunk=chunk, half=half):
                e.tensor_copy(Vp[0][:, chunk * 65:chunk * 65 + 64],
                              pg[1][:, half * 512:half * 512 + 64])
                e.tensor_copy(Vp[1][:, chunk * 65:chunk * 65 + 64],
                              pg[1][:, half * 512 + 64:half * 512 + 128])
                return flushv(e)
            P.emit("vector", vevac, waits=[vdone], incs=[(s_ve, 1)])
            v_psum_free[half] = wv(s_ve)
        hT_free[slot] = wv(s_mm)
    bar()

    P.emit("vector", lambda e: e.memset(mk2acc[:, :], 0.0), incs=[(s_ve, 1)])
    mm_prev = None
    for g in range(2):
        def k2op(e, g=g):
            e.activation(k2st[0:64, :], Kp[g][0:64, :], AF.Square)
            return flushs(e)
        P.emit("scalar", k2op, waits=[wv(s_ve)], incs=[(s_se, 1)])
        for sl in range(8):
            P.emit("tensor", lambda e, sl=sl: e.matmul(
                pg[3][0:1, 0:512], ones_bd[0:64, 0:1],
                k2st[0:64, sl * 512:(sl + 1) * 512], start=True, stop=True),
                waits=[wv(s_se), mm_prev], incs=[(s_mm, 1)])
            P.emit("vector", lambda e: e.reduce_max(mk2acc[0:1, 2:3],
                                                    pg[3][0:1, 0:512], axis=AX.X),
                   waits=[wv(s_mm)], incs=[(s_ve, 1)])
            P.emit("vector", lambda e, g=g: e.tensor_max(
                mk2acc[0:1, g:g + 1], mk2acc[0:1, g:g + 1], mk2acc[0:1, 2:3]),
                waits=[wv(s_ve)], incs=[(s_ve, 1)])
            mm_prev = wv(s_ve)
        P.emit("vector", flushv, incs=[(s_ve, 1)])
        P.dma("sync", mk2_s[g:g + 1, :], mk2acc[0:1, g:g + 1], s_wr,
              waits=[wv(s_ve)])
    for g in range(2):
        P.dma("sync", mk2bc[:, g:g + 1], mk2_s[g:g + 1, :].partition_broadcast(2),
              s_c, waits=[wv(s_wr)])
    bar()

    wq3 = U1.rearrange("p (d n) -> p d n", d=DC)
    for dc in range(DC):
        P.dma("gpsimd", wq3[:, dc, :], wqT_d[dc * 128:(dc + 1) * 128, :], s_w)
    def wqscale(e):
        for dc in range(DC):
            e.tensor_scalar_mul(wq3[:, dc, :], wq3[:, dc, :], cT[:, dc:dc + 1])
        return flushv(e)
    P.emit("vector", wqscale, waits=[wv(s_w)], incs=[(s_ve, 1)])
    wq_ready = wv(s_ve)
    P.dma("gpsimd", sd_bc[:, :], sd_d[:, :].partition_broadcast(128), s_bc)
    sd_ready = wv(s_bc)
    wd_ev = [None, None]

    def emit_wd(oc):
        ps = oc % 2
        P.dma("gpsimd", wdraw[ps], wd_d[oc * 128:(oc + 1) * 128, :], s_w,
              waits=[wd_ev[ps]])
        def wddeq(e, ps=ps):
            e.scalar_tensor_tensor(wdraw[ps], wdraw[ps], 128.0, sd_bc,
                                   op0=ALU.subtract, op1=ALU.mult)
            return flushv(e)
        P.emit("vector", wddeq,
               waits=[wv(s_w)] + ([sd_ready] if oc == 0 else []),
               incs=[(s_ve, 1)])
        wd_ev[ps] = P.dma("sync", wd_s[oc * 128:(oc + 1) * 128, :], wdraw[ps],
                          s_wr, waits=[wv(s_ve)])
    hT_free = [None, None]
    q_ev = [None, None]
    q_dma_ev = [None, None]
    m_dma_ev = None
    m_ev = None
    for tt in range(NTT):
        slot = tt % 2
        for dc in range(DC):
            P.dma("scalar", hT[slot][:, dc, :],
                  h_s[tt * 512:(tt + 1) * 512, dc * 128:(dc + 1) * 128],
                  s_t, transpose=True,
                  waits=[hT_free[slot] if dc == 0 else None])
        hT_w = wv(s_t)
        for oc in range(DC):
            half = oc % 2
            def qmm(e, slot=slot, oc=oc, half=half):
                last = None
                for ic in range(DC):
                    last = e.matmul(pg[0][:, half * 512:(half + 1) * 512],
                                    wq3[:, ic, oc * 128:(oc + 1) * 128],
                                    hT[slot][:, ic, :],
                                    start=(ic == 0), stop=(ic == DC - 1))
                return last
            P.emit("tensor", qmm, waits=[hT_w, wq_ready, q_ev[half]],
                   incs=[(s_mm, 1)])
            qdone = wv(s_mm)
            def qevac(e, half=half):
                e.copy(qt_st[half], pg[0][:, half * 512:(half + 1) * 512])
                e.activation(q2_st[half],
                             pg[0][:, half * 512:(half + 1) * 512],
                             AF.Square)
                return e.activation(spad[:, 0:32], spad[:, 0:32], AF.Square)
            P.emit("scalar", qevac, waits=[qdone, q_dma_ev[half]],
                   incs=[(s_se, 1)])
            q_ev[half] = wv(s_se)
            q_dma_ev[half] = P.dma(
                "sync", q_s[oc * 128:(oc + 1) * 128, tt * 512:(tt + 1) * 512],
                qt_st[half], s_q, waits=[q_ev[half]])
            P.emit("tensor", lambda e, half=half: e.matmul(
                pg[3][0:2, 0:512], ones_bd[:, :], q2_st[half],
                start=True, stop=True),
                waits=[q_ev[half], m_ev], incs=[(s_mm, 1)])
            g = oc // 8
            def mhat(e, g=g):
                e.activation(mneg_st[0:2, :], pg[3][0:2, 0:512], AF.Ln,
                             scale=mk2bc[:, g:g + 1])
                e.activation(mneg_st[0:2, :], mneg_st[0:2, :],
                             AF.Exp, scale=0.5)
                return flushs(e)
            P.emit("scalar", mhat, waits=[wv(s_mm)], incs=[(s_se, 1)])
            def mnegop(e):
                e.tensor_scalar_mul(mneg2[0:2, :], mneg_st[0:2, :], -1.0)
                return flushv(e)
            P.emit("vector", mnegop, waits=[wv(s_se), m_dma_ev],
                   incs=[(s_ve, 1)])
            m_ev = wv(s_ve)
            P.dma("gpsimd", m_s[2 * oc:2 * oc + 1, tt * 512:(tt + 1) * 512],
                  mneg2[0:1, :], s_m, waits=[m_ev])
            m_dma_ev = P.dma(
                "gpsimd", m_s[2 * oc + 1:2 * oc + 2, tt * 512:(tt + 1) * 512],
                mneg2[1:2, :], s_m, waits=[m_ev])
        hT_free[slot] = wv(s_mm)
        for oc in range(tt * 11, tt * 11 + 11):
            emit_wd(oc)
    bar()

    xnT = U1.rearrange("p (d n) -> p d n", d=DC)
    P.emit("scalar", lambda e: e.activation(spad[:, 0:64], spad[:, 64:128],
                                            AF.Silu), incs=[(s_se, 1)])
    for dc in range(DC):
        P.dma("scalar", xnT[:, dc, :],
              xn_s[:, dc * 128:(dc + 1) * 128], s_t, transpose=True)
    xnT_ready = wv(s_t)
    ring_free = [None, None]
    gu_free = [None] * 4
    gp_free = [None] * 4
    for ht in range(HTN):
        slot = ht % 2
        for dc in range(DC):
            P.dma("gpsimd", wg_ring[slot][:, dc, :],
                  gT_d[dc * 128:(dc + 1) * 128, ht * 512:(ht + 1) * 512], s_w,
                  waits=[ring_free[slot] if dc == 0 else None])
            P.dma("gpsimd", wu_ring[slot][:, dc, :],
                  uT_d[dc * 128:(dc + 1) * 128, ht * 512:(ht + 1) * 512], s_w)
        P.dma("sync", sbc_g[slot], sg_d[:, ht * 512:(ht + 1) * 512]
              .partition_broadcast(128), s_bc,
              waits=[ring_free[slot]])
        P.dma("sync", sbc_u[slot], su_d[:, ht * 512:(ht + 1) * 512]
              .partition_broadcast(128), s_bc)
        wload = [wv(s_w), wv(s_bc)]
        def deq(e, slot=slot):
            last = None
            for dc in range(DC):
                e.scalar_tensor_tensor(wg_ring[slot][:, dc, :],
                                       wg_ring[slot][:, dc, :], 128.0,
                                       sbc_g[slot],
                                       op0=ALU.subtract, op1=ALU.mult)
                last = e.scalar_tensor_tensor(wu_ring[slot][:, dc, :],
                                              wu_ring[slot][:, dc, :], 128.0,
                                              sbc_u[slot],
                                              op0=ALU.subtract, op1=ALU.mult)
            return last
        P.emit("vector", deq, waits=wload, incs=[(s_ve, 1)])
        deq_done = wv(s_ve)
        for tc in range(16):
            hs, tkt = tc // 4, tc % 4
            ph = tc % 4
            def gumm(e, slot=slot, hs=hs, tkt=tkt, ph=ph):
                last = None
                for dc in range(DC):
                    e.matmul(pg[ph][:, 0:512],
                             wg_ring[slot][:, dc, hs * 128:(hs + 1) * 128],
                             xnT[:, dc, tkt * 512:(tkt + 1) * 512],
                             start=(dc == 0), stop=(dc == DC - 1))
                    last = e.matmul(pg[ph][:, 512:1024],
                                    wu_ring[slot][:, dc, hs * 128:(hs + 1) * 128],
                                    xnT[:, dc, tkt * 512:(tkt + 1) * 512],
                                    start=(dc == 0), stop=(dc == DC - 1))
                return last
            P.emit("tensor", gumm,
                   waits=[deq_done, gp_free[ph],
                          xnT_ready if (ht == 0 and tc == 0) else None],
                   incs=[(s_mm, 1)])
            gdone = wv(s_mm)
            def siop(e, ph=ph):
                e.activation(silu_st[ph], pg[ph][:, 0:512], AF.Silu)
                return flushs(e)
            P.emit("scalar", siop, waits=[gdone], incs=[(s_se, 1)])
            gslot = tc % 4
            def guop(e, ph=ph, gslot=gslot):
                e.tensor_tensor(gu_st[gslot], silu_st[ph],
                                pg[ph][:, 512:1024], op=ALU.mult)
                return flushv(e)
            P.emit("vector", guop,
                waits=[wv(s_se), gu_free[gslot]], incs=[(s_ve, 1)])
            gp_free[ph] = wv(s_ve)
            gu_free[gslot] = P.dma(
                "sync", gu_s[ht * 512 + hs * 128:ht * 512 + (hs + 1) * 128,
                             tkt * 512:(tkt + 1) * 512],
                gu_st[gslot], s_wr, waits=[wv(s_ve)])
        ring_free[slot] = wv(s_mm)
    bar()

    wo3 = U1.rearrange("p (d n) -> p d n", d=DC)
    for dc in range(DC):
        P.dma("gpsimd", wo3[:, dc, :], woT_d[dc * 128:(dc + 1) * 128, :], s_w)
    wo_ready = wv(s_w)
    P.emit("scalar", lambda e: e.activation(spad[:, 0:64], spad[:, 64:128],
                                            AF.Exp), incs=[(s_se, 1)])
    def qp65z(e):
        for i in range(4):
            e.memset(qp_ring[i][64:66, :], 0.0)
        return flushv(e)
    P.emit("vector", qp65z, incs=[(s_ve, 1)])
    qp65_ready = wv(s_ve)
    ao_odd = U2b[:, 6144:6656]

    qp_free = [None] * 4
    qload_ev = [None] * 4
    pp_ev = [None, None, None]
    ev_ev = [None] * 4
    av_mm = [None, None]
    av_free = [None, None]
    probs_free = [None] * 4
    r_dma_ev = [None, None]
    ao_sh_ev = None
    guT_free = [None] * 4
    wdT_free = [None] * 4
    xre_free = [None, None]
    out_free = [None] * 4
    dpsum_free = None
    prev_dtail = None
    for tt in range(NTT):
        def emit_qload(hh, tt=tt):
            qs2 = hh % 4
            P.dma("sync", qp_ring[qs2][0:64, :],
                  q_s[hh * 64:(hh + 1) * 64, tt * 512:(tt + 1) * 512], s_q,
                  waits=[qp_free[qs2],
                         qp65_ready if (tt == 0 and hh < 4) else None])
            qload_ev[qs2] = P.dma(
                "sync", qp_ring[qs2][64:65, :],
                m_s[hh:hh + 1, tt * 512:(tt + 1) * 512], s_q)
        emit_qload(0)
        emit_qload(1)
        ao_ev_even = [None] * 16
        ao_ev_odd = [None] * 16
        for h in range(NH):
            g = h // 16
            qs = h % 4
            if h + 2 < NH:
                emit_qload(h + 2)
            qw = qload_ev[qs]
            ah = h % 2
            def mk_avmm(p, g=g, ah=ah):
                pb = 2048 + (p % 4) * 1024
                def avmm(e, p=p, pb=pb):
                    e.matmul(pgd[0:65, 3072 + ah * 512:3072 + (ah + 1) * 512],
                             Vp[g][:, (2 * p) * 65:(2 * p) * 65 + 65],
                             U2b[:, pb:pb + 512],
                             start=(p == 0), stop=False)
                    return e.matmul(pgd[0:65, 3072 + ah * 512:3072 + (ah + 1) * 512],
                                    Vp[g][:, (2 * p + 1) * 65:(2 * p + 1) * 65 + 65],
                                    U2b[:, pb + 512:pb + 1024],
                                    start=False, stop=(p == 15))
                return avmm
            for p in range(16):
                pr = pgd[:, (p % 3) * 1024:(p % 3 + 1) * 1024]
                def smm(e, g=g, qs=qs, p=p, pr=pr):
                    e.matmul(pr[:, 0:512],
                             Kp[g][0:66, (2 * p) * 128:(2 * p + 1) * 128],
                             qp_ring[qs][0:66, :], start=True, stop=True)
                    return e.matmul(pr[:, 512:1024],
                                    Kp[g][0:66, (2 * p + 1) * 128:(2 * p + 2) * 128],
                                    qp_ring[qs][0:66, :], start=True, stop=True)
                P.emit("tensor", smm,
                       waits=[qw if p == 0 else None, pp_ev[p % 3],
                              wo_ready if (tt == 0 and h == 0 and p == 0) else None,
                              qp65_ready if (tt == 0 and h < 4 and p == 0) else None,
                              prev_dtail if (h == 0 and p == 0) else None],
                       incs=[(s_mm, 1)])
                sdone = wv(s_mm)
                if p >= 2:
                    P.emit("tensor", mk_avmm(p - 2),
                           waits=[ev_ev[(p - 2) % 4],
                                  av_free[ah] if p == 2 else None],
                           incs=[(s_mm, 1)])
                    probs_free[(p - 2) % 4] = wv(s_mm)
                P.emit("scalar", lambda e, p=p, pr=pr: e.activation(
                    U2b[:, 2048 + (p % 4) * 1024:2048 + (p % 4) * 1024 + 1024],
                    pr[:, :], AF.Exp, scale=0.125),
                    waits=[sdone, probs_free[p % 4]], incs=[(s_se, 1)])
                pp_ev[p % 3] = wv(s_se)
                ev_ev[p % 4] = wv(s_se)
            for p in (14, 15):
                P.emit("tensor", mk_avmm(p), waits=[ev_ev[p % 4]],
                       incs=[(s_mm, 1)])
                probs_free[p % 4] = wv(s_mm)
            av_mm[ah] = wv(s_mm)
            hd2 = h // 2
            def recipop(e, ah=ah):
                e.reciprocal(r_st[ah][64:65, :],
                             pgd[64:65, 3072 + ah * 512:3072 + (ah + 1) * 512])
                return flushv(e)
            P.emit("vector", recipop, waits=[av_mm[ah], r_dma_ev[ah]],
                   incs=[(s_ve, 1)])
            r_dma_ev[ah] = P.dma(
                "gpsimd", r_s[h:h + 1, tt * 512:(tt + 1) * 512],
                r_st[ah][64:65, :], s_wr, waits=[wv(s_ve)])
            P.dma("gpsimd", rb_st[ah][0:64, :],
                  r_s[h:h + 1, tt * 512:(tt + 1) * 512].partition_broadcast(64),
                  s_r, waits=[wv(s_wr), av_free[ah]])
            if ah == 0:
                def aomul(e, hd2=hd2):
                    e.tensor_tensor(aot_sb[0:64, hd2 * 512:(hd2 + 1) * 512],
                                    pgd[0:64, 3072:3584],
                                    rb_st[0][0:64, :], op=ALU.mult)
                    return flushv(e)
                P.emit("vector", aomul, waits=[wv(s_r)], incs=[(s_ve, 1)])
                av_free[0] = wv(s_ve)
                ao_ev_even[hd2] = wv(s_ve)
            else:
                def aomul(e):
                    e.tensor_tensor(ao_odd[0:64, :],
                                    pgd[0:64, 3584:4096],
                                    rb_st[1][0:64, :], op=ALU.mult)
                    return flushv(e)
                P.emit("vector", aomul, waits=[wv(s_r), ao_sh_ev],
                       incs=[(s_ve, 1)])
                av_free[1] = wv(s_ve)
                ao_sh_ev = P.dma(
                    "gpsimd", aot_sb[64:128, hd2 * 512:(hd2 + 1) * 512],
                    ao_odd[0:64, :], s_ao, waits=[wv(s_ve)])
                ao_ev_odd[hd2] = ao_sh_ev
            qp_free[qs] = av_mm[ah]
        c_tail = av_free[1]

        ldg_ev = {}
        ldw_ev = {}

        def emit_dload(hc, half, tt=tt):
            gs = hc % 4
            P.dma("gpsimd", guT[gs],
                  gu_s[hc * 128:(hc + 1) * 128, tt * 512:(tt + 1) * 512],
                  s_gu, waits=[guT_free[gs]])
            ldg_ev[hc] = wv(s_gu)
            P.dma("gpsimd", wdT[gs],
                  wd_s[hc * 128:(hc + 1) * 128,
                       half * 1024:(half + 1) * 1024],
                  s_wd, waits=[wdT_free[gs]])
            ldw_ev[hc] = wv(s_wd)
        for half in range(2):
            emit_dload(0, half)
            emit_dload(1, half)
            for hc in range(HCN):
                gs = hc % 4
                if hc + 2 < HCN:
                    emit_dload(hc + 2, half)
                elif hc + 2 == HCN:
                    P.dma("gpsimd", jx[:, 0:4], x_d[0:128, 0:4], s_gu)
                    ldg_ev[HCN] = wv(s_gu)
                    P.dma("gpsimd", jx[:, 4:8], x_d[0:128, 4:8], s_wd)
                    ldw_ev[HCN] = wv(s_wd)
                def dmm(e, gs=gs, hc=hc):
                    last = None
                    for tc in range(4):
                        for ot in range(2):
                            p = tc * 2 + ot
                            last = e.matmul(
                                pg[p // 2][:, (p % 2) * 512:(p % 2 + 1) * 512],
                                guT[gs][:, tc * 128:(tc + 1) * 128],
                                wdT[gs][:, ot * 512:(ot + 1) * 512],
                                start=(hc == 0), stop=False)
                    return last
                P.emit("tensor", dmm,
                       waits=[ldg_ev[hc + 1], ldw_ev[hc + 1]]
                       + ([dpsum_free, c_tail] if hc == 0 else []),
                       incs=[(s_mm, 1)])
                guT_free[gs] = wv(s_mm)
                wdT_free[gs] = wv(s_mm)
            for hd in range(DC):
                def omm(e, hd=hd, half=half):
                    last = None
                    for tc in range(4):
                        for ot in range(2):
                            p = tc * 2 + ot
                            last = e.matmul(
                                pg[p // 2][:, (p % 2) * 512:(p % 2 + 1) * 512],
                                aot_sb[:, hd * 512 + tc * 128:
                                       hd * 512 + (tc + 1) * 128],
                                wo3[:, hd, half * 1024 + ot * 512:
                                    half * 1024 + (ot + 1) * 512],
                                start=False, stop=(hd == DC - 1))
                    return last
                P.emit("tensor", omm,
                       waits=[ao_ev_even[hd], ao_ev_odd[hd]] if half == 0 else [],
                       incs=[(s_mm, 1)])
            dmm_done = wv(s_mm)
            xre_ev = {}

            def emit_xload(tc, half=half, tt=tt):
                xs = tc % 2
                P.dma("gpsimd", xre[xs],
                      x_d[tt * 512 + tc * 128:tt * 512 + (tc + 1) * 128,
                          half * 1024:(half + 1) * 1024],
                      s_x, waits=[xre_free[xs]])
                xre_ev[tc] = wv(s_x)
            emit_xload(0)
            for tc in range(4):
                xs = tc % 2
                if tc + 1 < 4:
                    emit_xload(tc + 1)
                else:
                    P.dma("gpsimd", jx[:, 0:4], x_d[0:128, 0:4], s_x)
                    xre_ev[4] = wv(s_x)
                xw2 = xre_ev[tc + 1]
                ti = tt * 4 + tc
                for ot in range(2):
                    p = tc * 2 + ot
                    os_ = p % 4
                    def ev(e, p=p, ot=ot, xs=xs, ti=ti, os_=os_):
                        e.tensor_scalar(outst[os_], xre[xs][:, ot * 512:(ot + 1) * 512],
                                        stats_sb[:, 2 * ti:2 * ti + 1],
                                        stats_sb[:, 2 * ti + 1:2 * ti + 2],
                                        op0=ALU.mult, op1=ALU.add)
                        e.tensor_add(
                            outst[os_], outst[os_],
                            pg[p // 2][:, (p % 2) * 512:(p % 2 + 1) * 512])
                        return flushv(e)
                    P.emit("vector", ev, waits=[dmm_done, xw2, out_free[os_]],
                           incs=[(s_ve, 1)])
                    out_free[os_] = P.dma(
                        "sync",
                        out_d[tt * 512 + tc * 128:tt * 512 + (tc + 1) * 128,
                              half * 1024 + ot * 512:half * 1024 + (ot + 1) * 512],
                        outst[os_], s_st, waits=[wv(s_ve)])
                xre_free[xs] = wv(s_ve)
            dpsum_free = wv(s_ve)
        prev_dtail = dpsum_free

    P.emit("sync", lambda e: e.nop(), waits=[wv(s_st)])

    with nc.Block() as block:
        def mk(name):
            def f(eng):
                for (e, fn, waits, incs) in P.ops:
                    if e != name:
                        continue
                    for (h, v) in waits:
                        eng.wait_ge(h, v)
                    inst = fn(eng)
                    for (h, amt) in incs:
                        inst = inst.then_inc(h, amt)
            return f
        block.tensor(mk("tensor"))
        block.vector(mk("vector"))
        block.scalar(mk("scalar"))
        block.sync(mk("sync"))
        block.gpsimd(mk("gpsimd"))
    ES.close()
    return nc


_NC_CACHE = None


def kernel(L0_attn_x, L0_attn_norm_scale, L0_attn_split_scale, L0_attn_Wq,
           L0_attn_Wk, L0_attn_Wv, L0_attn_Wo, L0_ffn_wq_gate, L0_ffn_s_gate,
           L0_ffn_wq_up, L0_ffn_s_up, L0_ffn_wq_down, L0_ffn_s_down):
    global _NC_CACHE
    x = np.asarray(L0_attn_x, np.float32)
    wqT = np.ascontiguousarray(np.asarray(L0_attn_Wq, np.float32).T)
    wkT = np.ascontiguousarray(np.asarray(L0_attn_Wk, np.float32).T)
    wvT = np.ascontiguousarray(np.asarray(L0_attn_Wv, np.float32).T)
    woT = np.ascontiguousarray(np.asarray(L0_attn_Wo, np.float32).T)
    gT = np.ascontiguousarray(np.asarray(L0_ffn_wq_gate, np.int32).T)
    uT = np.ascontiguousarray(np.asarray(L0_ffn_wq_up, np.int32).T)
    wd = np.ascontiguousarray(np.asarray(L0_ffn_wq_down, np.int32).T)
    sg = np.ascontiguousarray(np.asarray(L0_ffn_s_gate, np.float32).reshape(1, HID))
    su = np.ascontiguousarray(np.asarray(L0_ffn_s_up, np.float32).reshape(1, HID))
    sdr = np.ascontiguousarray(
        np.asarray(L0_ffn_s_down, np.float32).reshape(1, D))
    nsr = np.ascontiguousarray(np.asarray(L0_attn_norm_scale, np.float32)
                               .reshape(DC, 128).T)
    ssr = np.ascontiguousarray(np.asarray(L0_attn_split_scale, np.float32)
                               .reshape(DC, 128).T)
    obd = np.zeros((128, 2), ml_dtypes.bfloat16)
    obd[0:64, 0] = 1
    obd[64:128, 1] = 1

    if _NC_CACHE is None:
        _NC_CACHE = build_nc()
    nc = _NC_CACHE

    in_maps = []
    for c in range(8):
        b, halfc = c // 2, c % 2
        xb = x[b]
        if halfc == 0:
            xc = xb
        else:
            xc = np.concatenate([xb[2048:], xb[:2048]], axis=0)
        in_maps.append(dict(
            x=np.ascontiguousarray(xc), wqT=wqT, wkT=wkT, wvT=wvT, woT=woT,
            gT=gT, uT=uT, wd=wd, sg=sg, su=su, sdr=sdr, nsr=nsr, ssr=ssr,
            ones_bd=obd))
    res = run_bass_kernel_spmd(nc, in_maps, core_ids=list(range(8)))
    out = np.empty((4, S, D), np.float32)
    for c in range(8):
        b, halfc = c // 2, c % 2
        out[b, halfc * 2048:(halfc + 1) * 2048] = res.results[c]["out"]
    return out


# version 28
# speedup vs baseline: 1.0301x; 1.0301x over previous
import numpy as np
import ml_dtypes
from contextlib import ExitStack
import concourse.bass as bass
import concourse.mybir as mybir
from concourse.bass_utils import run_bass_kernel_spmd

F32 = mybir.dt.float32
BF16 = mybir.dt.bfloat16
I32 = mybir.dt.int32
AF = mybir.ActivationFunctionType
ALU = mybir.AluOpType
AX = mybir.AxisListType

D = 2048
HID = 5632
S = 4096
OWN = 2048
NH = 32
DC = D // 128
HCN = HID // 128
HTN = HID // 512
NTT = OWN // 512
NKC = S // 128
EPS_LN = 1e-5
EPS_RMS = 1e-8

ENGINES = ("tensor", "vector", "scalar", "sync", "gpsimd")
DEBUG = False


class Sem:
    def __init__(self, h):
        self.h = h
        self.n = 0


class Prog:

    def __init__(self):
        self.ops = []

    def emit(self, engine, fn, waits=(), incs=()):
        rec = []
        for (s, amt) in incs:
            s.n += amt
            rec.append((s.h, amt))
        ws = [w for w in waits if w is not None]
        self.ops.append((engine, fn, ws, rec))

    def dma(self, engine, out_ap, in_ap, sem, transpose=False, waits=()):
        def fn(e, o=out_ap, i=in_ap, t=transpose):
            if t:
                return e.dma_start(out=o, in_=i, transpose=True)
            return e.dma_start(out=o, in_=i)
        self.emit(engine, fn, waits=waits, incs=[(sem, 16)])
        return (sem.h, sem.n)

    def barrier(self, data_sems, prog_sems):
        targets = [(s.h, s.n) for s in data_sems if s.n > 0]
        for e in ENGINES:
            ps = prog_sems[e]
            self.emit(e, lambda eng: eng.nop(), incs=[(ps, 1)])
        ptargets = [(prog_sems[e].h, prog_sems[e].n) for e in ENGINES]
        for e in ENGINES:
            self.emit(e, lambda eng: eng.nop(), waits=ptargets + targets)


def build_nc():
    nc = bass.Bass("TRN2", target_bir_lowering=False)

    x_d = nc.dram_tensor("x", [S, D], F32, kind="ExternalInput").ap()
    wqT_d = nc.dram_tensor("wqT", [D, D], F32, kind="ExternalInput").ap()
    wkT_d = nc.dram_tensor("wkT", [D, 128], F32, kind="ExternalInput").ap()
    wvT_d = nc.dram_tensor("wvT", [D, 128], F32, kind="ExternalInput").ap()
    woT_d = nc.dram_tensor("woT", [D, D], F32, kind="ExternalInput").ap()
    gT_d = nc.dram_tensor("gT", [D, HID], I32, kind="ExternalInput").ap()
    uT_d = nc.dram_tensor("uT", [D, HID], I32, kind="ExternalInput").ap()
    wd_d = nc.dram_tensor("wd", [HID, D], I32, kind="ExternalInput").ap()
    sg_d = nc.dram_tensor("sg", [1, HID], F32, kind="ExternalInput").ap()
    su_d = nc.dram_tensor("su", [1, HID], F32, kind="ExternalInput").ap()
    sd_d = nc.dram_tensor("sdr", [1, D], F32, kind="ExternalInput").ap()
    ns_d = nc.dram_tensor("nsr", [128, DC], F32, kind="ExternalInput").ap()
    ss_d = nc.dram_tensor("ssr", [128, DC], F32, kind="ExternalInput").ap()
    obd_d = nc.dram_tensor("ones_bd", [128, 2], BF16, kind="ExternalInput").ap()
    out_d = nc.dram_tensor("out", [OWN, D], F32, kind="ExternalOutput").ap()

    KI = "ExternalOutput" if DEBUG else "Internal"
    xn_s = nc.dram_tensor("xn_s", [OWN, D], BF16, kind=KI).ap()
    h_s = nc.dram_tensor("h_s", [S, D], BF16, kind=KI).ap()
    gu_s = nc.dram_tensor("gu_s", [HID, OWN], BF16, kind=KI).ap()
    wd_s = nc.dram_tensor("wd_s", [HID, D], BF16, kind=KI).ap()
    q_s = nc.dram_tensor("q_s", [D, OWN], BF16, kind=KI).ap()
    m_s = nc.dram_tensor("m_s", [NH, OWN], BF16, kind=KI).ap()
    r_s = nc.dram_tensor("r_s", [NH, OWN], F32, kind=KI).ap()
    c_s = nc.dram_tensor("c_s", [1, D], F32, kind=KI).ap()
    mk2_s = nc.dram_tensor("mk2_s", [2, 1], F32, kind=KI).ap()
    dbg_s = nc.dram_tensor("dbg_s", [128, 64], F32, kind=KI).ap()

    ES = ExitStack()
    sbt = lambda name, shape, dt: ES.enter_context(nc.sbuf_tensor(name, shape, dt))

    U1 = sbt("U1", [128, DC * D], BF16)
    U2b = sbt("U2b", [128, 32768], BF16)
    U2a = sbt("U2a", [128, 8192], F32)
    Kp = [sbt(f"Kp{g}", [128, S], BF16) for g in range(2)]
    Vp = [sbt(f"Vp{g}", [128, NKC * 65], BF16) for g in range(2)]
    stats_sb = sbt("stats_sb", [128, 32], F32)
    stv = sbt("stv_tmp", [128, 48], F32)
    sts = sbt("sts_tmp", [128, 16], F32)
    vpad = sbt("vpad", [128, 32], F32)
    spad = sbt("spad", [128, 256], F32)
    junkv = sbt("junkv", [128, 512], F32)
    cst = sbt("cst", [128, 8], F32)
    ones_bd = sbt("ones_bdb", [128, 2], BF16)
    mk2bc = sbt("mk2bc", [2, 2], F32)
    mk2acc = sbt("mk2acc", [1, 4], F32)
    jx = sbt("jx", [128, 8], F32)

    x_ring = [U2a[:, i * 2048:(i + 1) * 2048] for i in range(2)]
    x3 = U2a[:, 4096:6144]
    xn_st = [U2b[:, 24576 + i * 2048:24576 + (i + 1) * 2048] for i in range(2)]
    h_st = [U2b[:, 28672 + i * 2048:28672 + (i + 1) * 2048] for i in range(2)]
    sqf = sbt("sqf", [128, 2048], F32)
    stB = sbt("stB", [128, 320], F32)
    cT = sbt("cT", [128, DC], F32)
    hT = [U2b[:, i * 8192:(i + 1) * 8192].rearrange("p (d n) -> p d n", d=DC)
          for i in range(2)]
    k2st = U2b[:, 16384:20480]
    wkv = U2b[:, 20480:24576].rearrange("p (d n) -> p d n", d=DC)
    qt_st = [U2b[:, 24576 + i * 512:24576 + (i + 1) * 512] for i in range(2)]
    q2_st = [U2b[:, 25600 + i * 512:25600 + (i + 1) * 512] for i in range(2)]
    wdraw = [U2b[:, 16384:18432], U2b[:, 26624:28672]]
    sd_bc = U2a[:, 6144:8192]
    mneg_st = U2a[:, 0:512]
    mneg2 = U2a[:, 512:1024]
    wg_ring = [U2b[:, i * 8192:(i + 1) * 8192].rearrange("p (d n) -> p d n", d=DC)
               for i in range(2)]
    wu_ring = [U2b[:, 16384 + i * 8192:16384 + (i + 1) * 8192].rearrange(
        "p (d n) -> p d n", d=DC) for i in range(2)]
    sbc_g = [U2a[:, i * 512:(i + 1) * 512] for i in range(2)]
    sbc_u = [U2a[:, 1024 + i * 512:1024 + (i + 1) * 512] for i in range(2)]
    silu_st = [U2a[:, 2048 + i * 512:2048 + (i + 1) * 512] for i in range(2)]
    gu_sb = sbt("gu_sb", [128, 2048], BF16)
    gu_st = [gu_sb[:, i * 512:(i + 1) * 512] for i in range(4)]
    qp_ring = [U2b[:, i * 512:(i + 1) * 512] for i in range(4)]
    probs = [U2b[:, 2048 + i * 1024:2048 + (i + 1) * 1024] for i in range(4)]
    aot_sb = U2b[:, 18432:26624]
    r_st = [U2a[:, i * 512:(i + 1) * 512] for i in range(2)]
    rb_st = [U2a[:, 1024 + i * 512:1024 + (i + 1) * 512] for i in range(2)]
    guT = [U2b[:, 8192 + i * 512:8192 + (i + 1) * 512] for i in range(4)]
    wdT = [U2b[:, 12288 + i * 1024:12288 + (i + 1) * 1024] for i in range(4)]
    xre = [U2a[:, 2048 + i * 1024:2048 + (i + 1) * 1024] for i in range(2)]
    outst = [U2a[:, 4096 + i * 512:4096 + (i + 1) * 512] for i in range(4)]

    pgd = ES.enter_context(nc.psum_tensor("pgd", [128, 4096], F32))
    pg = [pgd[:, i * 1024:(i + 1) * 1024] for i in range(4)]

    names = ["s_x", "s_c", "s_w", "s_t", "s_wr", "s_q", "s_m", "s_r", "s_ao",
             "s_gu", "s_wd", "s_st", "s_mm", "s_ve", "s_se", "s_bc",
             "p_t", "p_v", "p_s", "p_y", "p_g"]
    SM = {n: Sem(ES.enter_context(nc.semaphore(n))) for n in names}
    prog_sems = {"tensor": SM["p_t"], "vector": SM["p_v"], "scalar": SM["p_s"],
                 "sync": SM["p_y"], "gpsimd": SM["p_g"]}
    data_sems = [SM[n] for n in names if not n.startswith("p_")]
    s_x, s_c, s_w, s_t, s_wr = (SM[n] for n in ["s_x", "s_c", "s_w", "s_t", "s_wr"])
    s_q, s_m, s_r, s_ao = (SM[n] for n in ["s_q", "s_m", "s_r", "s_ao"])
    s_gu, s_wd, s_st = (SM[n] for n in ["s_gu", "s_wd", "s_st"])
    s_mm, s_ve, s_se, s_bc = (SM[n] for n in ["s_mm", "s_ve", "s_se", "s_bc"])

    P = Prog()
    wv = lambda s: (s.h, s.n)
    bar = lambda: P.barrier(data_sems, prog_sems)

    nc.const_aps.register(F32, EPS_RMS, cst[:, 0:1])
    P.emit("vector", lambda e: e.memset(cst[:, 0:1], EPS_RMS), incs=[(s_ve, 1)])
    P.emit("vector", lambda e: e.memset(junkv[:, :], 0.0), incs=[(s_ve, 1)])
    P.emit("vector", lambda e: e.memset(vpad[:, :], 0.0), incs=[(s_ve, 1)])
    P.emit("vector", lambda e: e.memset(stv[:, :], 1.0), incs=[(s_ve, 1)])
    def swarm(e):
        e.memzero(spad[:, :])
        e.activation(spad[:, 0:64], spad[:, 64:128], AF.Square)
        e.activation(spad[:, 0:64], spad[:, 64:128], AF.Exp)
        return e.activation(spad[:, 0:64], spad[:, 0:64], AF.Ln)
    P.emit("scalar", swarm, incs=[(s_se, 1)])
    flushv = lambda e: e.tensor_copy(junkv[:, :], junkv[:, :])
    flushn = lambda e: e.tensor_copy(vpad[:, :], vpad[:, :])
    flushs = lambda e: e.activation(spad[:, :], spad[:, :], AF.Square)
    P.dma("sync", cT[:, :], ns_d[:, :], s_c)
    P.dma("sync", stB[:, 0:DC], ss_d[:, :], s_c)
    P.dma("sync", ones_bd[:, :], obd_d[:, :], s_c)
    def cmul(e):
        e.tensor_mul(cT[:, :], cT[:, :], stB[:, 0:DC])
        return flushv(e)
    P.emit("vector", cmul, waits=[wv(s_c)], incs=[(s_ve, 1)])
    cT_ready = wv(s_ve)
    c_spilled = cT_ready
    for dc in range(DC):
        P.dma("gpsimd", wkv[:, dc, 0:128], wkT_d[dc * 128:(dc + 1) * 128, :], s_w)
        P.dma("gpsimd", wkv[:, dc, 128:256], wvT_d[dc * 128:(dc + 1) * 128, :], s_w)
    def wkvscale(e):
        for dc in range(DC):
            e.tensor_scalar_mul(wkv[:, dc, :], wkv[:, dc, :], cT[:, dc:dc + 1])
        return flushv(e)
    P.emit("vector", wkvscale, waits=[wv(s_w), cT_ready], incs=[(s_ve, 1)])
    wkv_ready = wv(s_ve)

    invD = 1.0 / D
    SUM = stB[:, 0:32]
    SSQ = stB[:, 32:64]
    MEAN = stB[:, 64:96]
    VAR = stB[:, 96:128]
    VPE = stB[:, 128:160]
    A_ = stB[:, 160:192]
    NRM = stB[:, 192:224]
    R_ = stB[:, 224:256]
    AR_ = stB[:, 256:288]
    MA_ = stB[:, 288:320]
    sqv2 = [sqf, x3]
    l1_free = [None, None]
    sq_free = [None, None]
    for i in range(32):
        slot = i % 2
        xr = x_ring[slot]
        P.dma("sync", xr, x_d[i * 128:(i + 1) * 128, :], s_x,
              waits=[l1_free[slot]])
        xw = wv(s_x)
        P.emit("vector", lambda e, xr=xr, i=i: e.reduce_sum(
            SUM[:, i:i + 1], xr, axis=AX.X), waits=[xw], incs=[(s_ve, 1)])
        P.emit("scalar", lambda e, xr=xr, slot=slot: e.activation(
            sqv2[slot][:, :], xr, AF.Square),
            waits=[xw, sq_free[slot], c_spilled if i == 1 else None],
            incs=[(s_se, 1)])
        P.emit("vector", lambda e, i=i, slot=slot: e.reduce_sum(
            SSQ[:, i:i + 1], sqv2[slot][:, :], axis=AX.X),
            waits=[wv(s_se)], incs=[(s_ve, 1)])
        sq_free[slot] = wv(s_ve)
        l1_free[slot] = wv(s_ve)
    sq2_last = sq_free[1]
    def bemit(fn, waits=None):
        def g(e, fn=fn):
            fn(e)
            return flushv(e)
        P.emit("vector", g, waits=[wv(s_ve)] if waits is None else waits,
               incs=[(s_ve, 1)])
        return wv(s_ve)
    bemit(lambda e: e.nop())
    bemit(lambda e: (e.tensor_scalar_mul(MEAN, SUM, invD),
                     e.tensor_scalar_mul(VAR, SSQ, invD)))
    bemit(lambda e: e.tensor_mul(VPE, MEAN, MEAN))
    bemit(lambda e: e.tensor_sub(VAR, VAR, VPE))
    bemit(lambda e: e.tensor_scalar_add(VPE, VAR, EPS_LN))
    def sq_batch2(e):
        e.activation(NRM, VPE, AF.Sqrt)
        e.activation(R_, VAR, AF.Sqrt)
        return flushs(e)
    P.emit("scalar", sq_batch2, waits=[wv(s_ve)], incs=[(s_se, 1)])
    bemit(lambda e: e.reciprocal(A_, NRM), waits=[wv(s_se)])
    bemit(lambda e: e.tensor_mul(NRM, R_, A_))
    bemit(lambda e: e.tensor_scalar_add(NRM, NRM, EPS_RMS))
    bemit(lambda e: e.reciprocal(R_, NRM))
    bemit(lambda e: e.tensor_mul(AR_, R_, A_))
    bemit(lambda e: (e.tensor_mul(MA_, MEAN, A_),
                     e.tensor_scalar_add(stats_sb[:, 0:32:2], A_[:, 0:16],
                                         1.0)))
    bdone = bemit(lambda e: e.tensor_scalar_mul(stats_sb[:, 1:32:2],
                                                MA_[:, 0:16], -1.0))
    h_ev = []
    xr3 = [x_ring[0], x_ring[1], x3]
    l2_free = [l1_free[0], l1_free[1], sq2_last]
    a1_st_free = [None, None]
    for i in range(32):
        own = i < 16
        slot = i % 3
        st = i % 2
        xr = xr3[slot]
        P.dma("gpsimd", xr, x_d[i * 128:(i + 1) * 128, :], s_x,
              waits=[l2_free[slot]])
        xw = wv(s_x)
        def vap(e, i=i, xr=xr, st=st, own=own):
            if own:
                e.tensor_scalar(xn_st[st], xr, MEAN[:, i:i + 1], A_[:, i:i + 1],
                                op0=ALU.subtract, op1=ALU.mult)
            e.tensor_scalar(h_st[st], xr, MEAN[:, i:i + 1], AR_[:, i:i + 1],
                            op0=ALU.subtract, op1=ALU.mult)
            return flushv(e)
        P.emit("vector", vap,
               waits=[xw, a1_st_free[st], bdone if i == 0 else None],
               incs=[(s_ve, 1)])
        vv = wv(s_ve)
        l2_free[slot] = vv
        if own:
            P.dma("sync", xn_s[i * 128:(i + 1) * 128, :], xn_st[st], s_wr,
                  waits=[vv])
        last = P.dma("sync", h_s[i * 128:(i + 1) * 128, :], h_st[st], s_wr,
                     waits=[vv])
        a1_st_free[st] = last
        h_ev.append(last)
    bar()

    P.emit("vector", lambda e: e.memset(Kp[0][64:66, :], 0.0), incs=[(s_ve, 1)])
    P.emit("vector", lambda e: e.memset(Kp[1][64:66, :], 0.0), incs=[(s_ve, 1)])
    P.emit("vector", lambda e: e.memset(Kp[0][64:65, :], 1.0), incs=[(s_ve, 1)])
    P.emit("vector", lambda e: e.memset(Kp[1][64:65, :], 1.0), incs=[(s_ve, 1)])
    P.emit("vector", lambda e: e.memset(Vp[0][:, 64::65], 1.0), incs=[(s_ve, 1)])
    def vmemset4(e):
        e.memset(Vp[1][:, 64::65], 1.0)
        return flushv(e)
    P.emit("vector", vmemset4, incs=[(s_ve, 1)])

    hT_free = [None, None]
    kv_psum_free = None
    v_psum_free = [None, None]
    for tt in range(8):
        slot = tt % 2
        for dc in range(DC):
            P.dma("scalar", hT[slot][:, dc, :],
                  h_s[tt * 512:(tt + 1) * 512, dc * 128:(dc + 1) * 128],
                  s_t, transpose=True,
                  waits=[hT_free[slot] if dc == 0 else None,
                         h_ev[4 * tt + 3] if dc == 0 else None])
        hT_w = wv(s_t)

        def kmm(e, slot=slot):
            last = None
            for g in range(2):
                for dc in range(DC):
                    last = e.matmul(pg[0][0:64, g * 512:(g + 1) * 512],
                                    wkv[:, dc, g * 64:g * 64 + 64],
                                    hT[slot][:, dc, :],
                                    start=(dc == 0), stop=(dc == DC - 1))
            return last
        P.emit("tensor", kmm, waits=[hT_w, wkv_ready, kv_psum_free],
               incs=[(s_mm, 1)])
        kdone = wv(s_mm)
        def kevac(e, tt=tt):
            e.tensor_copy(Kp[0][0:64, tt * 512:(tt + 1) * 512], pg[0][0:64, 0:512])
            e.tensor_copy(Kp[1][0:64, tt * 512:(tt + 1) * 512],
                          pg[0][0:64, 512:1024])
            return flushv(e)
        P.emit("vector", kevac, waits=[kdone], incs=[(s_ve, 1)])
        kv_psum_free = wv(s_ve)
        for sub in range(4):
            half = sub % 2
            def vmm(e, slot=slot, sub=sub, half=half):
                last = None
                for dc in range(DC):
                    last = e.matmul(pg[1][:, half * 512:half * 512 + 128],
                                    hT[slot][:, dc, sub * 128:(sub + 1) * 128],
                                    wkv[:, dc, 128:256],
                                    start=(dc == 0), stop=(dc == DC - 1))
                return last
            P.emit("tensor", vmm, waits=[v_psum_free[half]], incs=[(s_mm, 1)])
            vdone = wv(s_mm)
            chunk = tt * 4 + sub
            def vevac(e, chunk=chunk, half=half):
                e.tensor_copy(Vp[0][:, chunk * 65:chunk * 65 + 64],
                              pg[1][:, half * 512:half * 512 + 64])
                e.tensor_copy(Vp[1][:, chunk * 65:chunk * 65 + 64],
                              pg[1][:, half * 512 + 64:half * 512 + 128])
                return flushv(e)
            P.emit("vector", vevac, waits=[vdone], incs=[(s_ve, 1)])
            v_psum_free[half] = wv(s_ve)
        hT_free[slot] = wv(s_mm)
    bar()

    P.emit("vector", lambda e: e.memset(mk2acc[:, :], 0.0), incs=[(s_ve, 1)])
    mm_prev = None
    for g in range(2):
        def k2op(e, g=g):
            e.activation(k2st[0:64, :], Kp[g][0:64, :], AF.Square)
            return flushs(e)
        P.emit("scalar", k2op, waits=[wv(s_ve)], incs=[(s_se, 1)])
        for sl in range(8):
            P.emit("tensor", lambda e, sl=sl: e.matmul(
                pg[3][0:1, 0:512], ones_bd[0:64, 0:1],
                k2st[0:64, sl * 512:(sl + 1) * 512], start=True, stop=True),
                waits=[wv(s_se), mm_prev], incs=[(s_mm, 1)])
            P.emit("vector", lambda e: e.reduce_max(mk2acc[0:1, 2:3],
                                                    pg[3][0:1, 0:512], axis=AX.X),
                   waits=[wv(s_mm)], incs=[(s_ve, 1)])
            P.emit("vector", lambda e, g=g: e.tensor_max(
                mk2acc[0:1, g:g + 1], mk2acc[0:1, g:g + 1], mk2acc[0:1, 2:3]),
                waits=[wv(s_ve)], incs=[(s_ve, 1)])
            mm_prev = wv(s_ve)
        P.emit("vector", flushv, incs=[(s_ve, 1)])
        P.dma("sync", mk2_s[g:g + 1, :], mk2acc[0:1, g:g + 1], s_wr,
              waits=[wv(s_ve)])
    for g in range(2):
        P.dma("sync", mk2bc[:, g:g + 1], mk2_s[g:g + 1, :].partition_broadcast(2),
              s_c, waits=[wv(s_wr)])
    bar()

    wq3 = U1.rearrange("p (d n) -> p d n", d=DC)
    for dc in range(DC):
        P.dma("gpsimd", wq3[:, dc, :], wqT_d[dc * 128:(dc + 1) * 128, :], s_w)
    def wqscale(e):
        for dc in range(DC):
            e.tensor_scalar_mul(wq3[:, dc, :], wq3[:, dc, :], cT[:, dc:dc + 1])
        return flushv(e)
    P.emit("vector", wqscale, waits=[wv(s_w)], incs=[(s_ve, 1)])
    wq_ready = wv(s_ve)
    P.dma("gpsimd", sd_bc[:, :], sd_d[:, :].partition_broadcast(128), s_bc)
    sd_ready = wv(s_bc)
    wd_ev = [None, None]

    def emit_wd(oc):
        ps = oc % 2
        P.dma("gpsimd", wdraw[ps], wd_d[oc * 128:(oc + 1) * 128, :], s_w,
              waits=[wd_ev[ps]])
        def wddeq(e, ps=ps):
            e.scalar_tensor_tensor(wdraw[ps], wdraw[ps], 128.0, sd_bc,
                                   op0=ALU.subtract, op1=ALU.mult)
            return flushv(e)
        P.emit("vector", wddeq,
               waits=[wv(s_w)] + ([sd_ready] if oc == 0 else []),
               incs=[(s_ve, 1)])
        wd_ev[ps] = P.dma("sync", wd_s[oc * 128:(oc + 1) * 128, :], wdraw[ps],
                          s_wr, waits=[wv(s_ve)])
    hT_free = [None, None]
    q_ev = [None, None]
    q_dma_ev = [None, None]
    m_dma_ev = None
    m_ev = None
    for tt in range(NTT):
        slot = tt % 2
        for dc in range(DC):
            P.dma("scalar", hT[slot][:, dc, :],
                  h_s[tt * 512:(tt + 1) * 512, dc * 128:(dc + 1) * 128],
                  s_t, transpose=True,
                  waits=[hT_free[slot] if dc == 0 else None])
        hT_w = wv(s_t)
        for oc in range(DC):
            half = oc % 2
            def qmm(e, slot=slot, oc=oc, half=half):
                last = None
                for ic in range(DC):
                    last = e.matmul(pg[0][:, half * 512:(half + 1) * 512],
                                    wq3[:, ic, oc * 128:(oc + 1) * 128],
                                    hT[slot][:, ic, :],
                                    start=(ic == 0), stop=(ic == DC - 1))
                return last
            P.emit("tensor", qmm, waits=[hT_w, wq_ready, q_ev[half]],
                   incs=[(s_mm, 1)])
            qdone = wv(s_mm)
            def qevac(e, half=half):
                e.copy(qt_st[half], pg[0][:, half * 512:(half + 1) * 512])
                e.activation(q2_st[half],
                             pg[0][:, half * 512:(half + 1) * 512],
                             AF.Square)
                return e.activation(spad[:, 0:32], spad[:, 0:32], AF.Square)
            P.emit("scalar", qevac, waits=[qdone, q_dma_ev[half]],
                   incs=[(s_se, 1)])
            q_ev[half] = wv(s_se)
            q_dma_ev[half] = P.dma(
                "sync", q_s[oc * 128:(oc + 1) * 128, tt * 512:(tt + 1) * 512],
                qt_st[half], s_q, waits=[q_ev[half]])
            P.emit("tensor", lambda e, half=half: e.matmul(
                pg[3][0:2, 0:512], ones_bd[:, :], q2_st[half],
                start=True, stop=True),
                waits=[q_ev[half], m_ev], incs=[(s_mm, 1)])
            g = oc // 8
            def mhat(e, g=g):
                e.activation(mneg_st[0:2, :], pg[3][0:2, 0:512], AF.Ln,
                             scale=mk2bc[:, g:g + 1])
                e.activation(mneg_st[0:2, :], mneg_st[0:2, :],
                             AF.Exp, scale=0.5)
                return flushs(e)
            P.emit("scalar", mhat, waits=[wv(s_mm)], incs=[(s_se, 1)])
            def mnegop(e):
                e.tensor_scalar_mul(mneg2[0:2, :], mneg_st[0:2, :], -1.0)
                return flushv(e)
            P.emit("vector", mnegop, waits=[wv(s_se), m_dma_ev],
                   incs=[(s_ve, 1)])
            m_ev = wv(s_ve)
            P.dma("gpsimd", m_s[2 * oc:2 * oc + 1, tt * 512:(tt + 1) * 512],
                  mneg2[0:1, :], s_m, waits=[m_ev])
            m_dma_ev = P.dma(
                "gpsimd", m_s[2 * oc + 1:2 * oc + 2, tt * 512:(tt + 1) * 512],
                mneg2[1:2, :], s_m, waits=[m_ev])
        hT_free[slot] = wv(s_mm)
        for oc in range(tt * 11, tt * 11 + 11):
            emit_wd(oc)
    bar()

    xnT = U1.rearrange("p (d n) -> p d n", d=DC)
    P.emit("scalar", lambda e: e.activation(spad[:, 0:64], spad[:, 64:128],
                                            AF.Silu), incs=[(s_se, 1)])
    for dc in range(DC):
        P.dma("scalar", xnT[:, dc, :],
              xn_s[:, dc * 128:(dc + 1) * 128], s_t, transpose=True)
    xnT_ready = wv(s_t)
    ring_free = [None, None]
    gu_free = [None] * 4
    gp_free = [None, None]
    for ht in range(HTN):
        slot = ht % 2
        for dc in range(DC):
            P.dma("gpsimd", wg_ring[slot][:, dc, :],
                  gT_d[dc * 128:(dc + 1) * 128, ht * 512:(ht + 1) * 512], s_w,
                  waits=[ring_free[slot] if dc == 0 else None])
            P.dma("gpsimd", wu_ring[slot][:, dc, :],
                  uT_d[dc * 128:(dc + 1) * 128, ht * 512:(ht + 1) * 512], s_w)
        P.dma("sync", sbc_g[slot], sg_d[:, ht * 512:(ht + 1) * 512]
              .partition_broadcast(128), s_bc,
              waits=[ring_free[slot]])
        P.dma("sync", sbc_u[slot], su_d[:, ht * 512:(ht + 1) * 512]
              .partition_broadcast(128), s_bc)
        wload = [wv(s_w), wv(s_bc)]
        def deq(e, slot=slot):
            last = None
            for dc in range(DC):
                e.scalar_tensor_tensor(wg_ring[slot][:, dc, :],
                                       wg_ring[slot][:, dc, :], 128.0,
                                       sbc_g[slot],
                                       op0=ALU.subtract, op1=ALU.mult)
                last = e.scalar_tensor_tensor(wu_ring[slot][:, dc, :],
                                              wu_ring[slot][:, dc, :], 128.0,
                                              sbc_u[slot],
                                              op0=ALU.subtract, op1=ALU.mult)
            return last
        P.emit("vector", deq, waits=wload, incs=[(s_ve, 1)])
        deq_done = wv(s_ve)
        for tc in range(16):
            hs, tkt = tc // 4, tc % 4
            ph = tc % 2
            def gumm(e, slot=slot, hs=hs, tkt=tkt, ph=ph):
                last = None
                for dc in range(DC):
                    e.matmul(pg[ph][:, 0:512],
                             wg_ring[slot][:, dc, hs * 128:(hs + 1) * 128],
                             xnT[:, dc, tkt * 512:(tkt + 1) * 512],
                             start=(dc == 0), stop=(dc == DC - 1))
                    last = e.matmul(pg[ph][:, 512:1024],
                                    wu_ring[slot][:, dc, hs * 128:(hs + 1) * 128],
                                    xnT[:, dc, tkt * 512:(tkt + 1) * 512],
                                    start=(dc == 0), stop=(dc == DC - 1))
                return last
            P.emit("tensor", gumm,
                   waits=[deq_done, gp_free[ph],
                          xnT_ready if (ht == 0 and tc == 0) else None],
                   incs=[(s_mm, 1)])
            gdone = wv(s_mm)
            def siop(e, ph=ph):
                e.activation(silu_st[ph], pg[ph][:, 0:512], AF.Silu)
                return flushs(e)
            P.emit("scalar", siop, waits=[gdone], incs=[(s_se, 1)])
            gslot = tc % 4
            def guop(e, ph=ph, gslot=gslot):
                e.tensor_tensor(gu_st[gslot], silu_st[ph],
                                pg[ph][:, 512:1024], op=ALU.mult)
                return flushv(e)
            P.emit("vector", guop,
                waits=[wv(s_se), gu_free[gslot]], incs=[(s_ve, 1)])
            gp_free[ph] = wv(s_ve)
            gu_free[gslot] = P.dma(
                "sync", gu_s[ht * 512 + hs * 128:ht * 512 + (hs + 1) * 128,
                             tkt * 512:(tkt + 1) * 512],
                gu_st[gslot], s_wr, waits=[wv(s_ve)])
        ring_free[slot] = wv(s_mm)
    bar()

    wo3 = U1.rearrange("p (d n) -> p d n", d=DC)
    for dc in range(DC):
        P.dma("gpsimd", wo3[:, dc, :], woT_d[dc * 128:(dc + 1) * 128, :], s_w)
    wo_ready = wv(s_w)
    P.emit("scalar", lambda e: e.activation(spad[:, 0:64], spad[:, 64:128],
                                            AF.Exp), incs=[(s_se, 1)])
    def qp65z(e):
        for i in range(4):
            e.memset(qp_ring[i][64:66, :], 0.0)
        return flushv(e)
    P.emit("vector", qp65z, incs=[(s_ve, 1)])
    qp65_ready = wv(s_ve)
    ao_odd = U2b[:, 6144:6656]

    qp_free = [None] * 4
    qload_ev = [None] * 4
    probs_addr = [2048 + i * 1024 for i in range(4)] \
        + [26624 + i * 1024 for i in range(4)]
    pp_ev = [None, None, None]
    ev_ev = [None] * 8
    av_mm = [None, None]
    av_free = [None, None]
    probs_free = [None] * 8
    r_dma_ev = [None, None]
    ao_sh_ev = None
    for tt in range(NTT):
        def emit_qload(hh, tt=tt):
            qs2 = hh % 4
            P.dma("sync", qp_ring[qs2][0:64, :],
                  q_s[hh * 64:(hh + 1) * 64, tt * 512:(tt + 1) * 512], s_q,
                  waits=[qp_free[qs2],
                         qp65_ready if (tt == 0 and hh < 4) else None])
            qload_ev[qs2] = P.dma(
                "sync", qp_ring[qs2][64:65, :],
                m_s[hh:hh + 1, tt * 512:(tt + 1) * 512], s_q)
        emit_qload(0)
        emit_qload(1)
        ao_ev_even = [None] * 16
        ao_ev_odd = [None] * 16
        for h in range(NH):
            g = h // 16
            qs = h % 4
            if h + 2 < NH:
                emit_qload(h + 2)
            qw = qload_ev[qs]
            ah = h % 2
            def mk_avmm(p, g=g, ah=ah):
                pb = probs_addr[p % 8]
                def avmm(e, p=p, pb=pb):
                    e.matmul(pgd[0:65, 3072 + ah * 512:3072 + (ah + 1) * 512],
                             Vp[g][:, (2 * p) * 65:(2 * p) * 65 + 65],
                             U2b[:, pb:pb + 512],
                             start=(p == 0), stop=False)
                    return e.matmul(pgd[0:65, 3072 + ah * 512:3072 + (ah + 1) * 512],
                                    Vp[g][:, (2 * p + 1) * 65:(2 * p + 1) * 65 + 65],
                                    U2b[:, pb + 512:pb + 1024],
                                    start=False, stop=(p == 15))
                return avmm
            for p in range(16):
                pr = pgd[:, (p % 3) * 1024:(p % 3 + 1) * 1024]
                def smm(e, g=g, qs=qs, p=p, pr=pr):
                    e.matmul(pr[:, 0:512],
                             Kp[g][0:66, (2 * p) * 128:(2 * p + 1) * 128],
                             qp_ring[qs][0:66, :], start=True, stop=True)
                    return e.matmul(pr[:, 512:1024],
                                    Kp[g][0:66, (2 * p + 1) * 128:(2 * p + 2) * 128],
                                    qp_ring[qs][0:66, :], start=True, stop=True)
                P.emit("tensor", smm,
                       waits=[qw if p == 0 else None, pp_ev[p % 3],
                              wo_ready if (tt == 0 and h == 0 and p == 0) else None,
                              qp65_ready if (tt == 0 and h < 4 and p == 0) else None],
                       incs=[(s_mm, 1)])
                sdone = wv(s_mm)
                if p >= 2:
                    P.emit("tensor", mk_avmm(p - 2),
                           waits=[ev_ev[(p - 2) % 8],
                                  av_free[ah] if p == 2 else None],
                           incs=[(s_mm, 1)])
                    probs_free[(p - 2) % 8] = wv(s_mm)
                P.emit("scalar", lambda e, p=p, pr=pr: e.activation(
                    U2b[:, probs_addr[p % 8]:probs_addr[p % 8] + 1024],
                    pr[:, :], AF.Exp, scale=0.125),
                    waits=[sdone, probs_free[p % 8]], incs=[(s_se, 1)])
                pp_ev[p % 3] = wv(s_se)
                ev_ev[p % 8] = wv(s_se)
            for p in (14, 15):
                P.emit("tensor", mk_avmm(p), waits=[ev_ev[p % 8]],
                       incs=[(s_mm, 1)])
                probs_free[p % 8] = wv(s_mm)
            av_mm[ah] = wv(s_mm)
            hd2 = h // 2
            def recipop(e, ah=ah):
                e.reciprocal(r_st[ah][64:65, :],
                             pgd[64:65, 3072 + ah * 512:3072 + (ah + 1) * 512])
                return flushv(e)
            P.emit("vector", recipop, waits=[av_mm[ah], r_dma_ev[ah]],
                   incs=[(s_ve, 1)])
            r_dma_ev[ah] = P.dma(
                "gpsimd", r_s[h:h + 1, tt * 512:(tt + 1) * 512],
                r_st[ah][64:65, :], s_wr, waits=[wv(s_ve)])
            P.dma("gpsimd", rb_st[ah][0:64, :],
                  r_s[h:h + 1, tt * 512:(tt + 1) * 512].partition_broadcast(64),
                  s_r, waits=[wv(s_wr), av_free[ah]])
            if ah == 0:
                def aomul(e, hd2=hd2):
                    e.tensor_tensor(aot_sb[0:64, hd2 * 512:(hd2 + 1) * 512],
                                    pgd[0:64, 3072:3584],
                                    rb_st[0][0:64, :], op=ALU.mult)
                    return flushv(e)
                P.emit("vector", aomul, waits=[wv(s_r)], incs=[(s_ve, 1)])
                av_free[0] = wv(s_ve)
                ao_ev_even[hd2] = wv(s_ve)
            else:
                def aomul(e):
                    e.tensor_tensor(ao_odd[0:64, :],
                                    pgd[0:64, 3584:4096],
                                    rb_st[1][0:64, :], op=ALU.mult)
                    return flushv(e)
                P.emit("vector", aomul, waits=[wv(s_r), ao_sh_ev],
                       incs=[(s_ve, 1)])
                av_free[1] = wv(s_ve)
                ao_sh_ev = P.dma(
                    "gpsimd", aot_sb[64:128, hd2 * 512:(hd2 + 1) * 512],
                    ao_odd[0:64, :], s_ao, waits=[wv(s_ve)])
                ao_ev_odd[hd2] = ao_sh_ev
            qp_free[qs] = av_mm[ah]
        c_tail = av_free[1]

        guT_free = [None] * 4
        wdT_free = [None] * 4
        xre_free = [None, None]
        out_free = [None] * 4
        dpsum_free = None
        ldg_ev = {}
        ldw_ev = {}

        def emit_dload(hc, half, tt=tt):
            gs = hc % 4
            P.dma("gpsimd", guT[gs],
                  gu_s[hc * 128:(hc + 1) * 128, tt * 512:(tt + 1) * 512],
                  s_gu, waits=[guT_free[gs]])
            ldg_ev[hc] = wv(s_gu)
            P.dma("gpsimd", wdT[gs],
                  wd_s[hc * 128:(hc + 1) * 128,
                       half * 1024:(half + 1) * 1024],
                  s_wd, waits=[wdT_free[gs]])
            ldw_ev[hc] = wv(s_wd)
        for half in range(2):
            emit_dload(0, half)
            emit_dload(1, half)
            for hc in range(HCN):
                gs = hc % 4
                if hc + 2 < HCN:
                    emit_dload(hc + 2, half)
                elif hc + 2 == HCN:
                    P.dma("gpsimd", jx[:, 0:4], x_d[0:128, 0:4], s_gu)
                    ldg_ev[HCN] = wv(s_gu)
                    P.dma("gpsimd", jx[:, 4:8], x_d[0:128, 4:8], s_wd)
                    ldw_ev[HCN] = wv(s_wd)
                def dmm(e, gs=gs, hc=hc):
                    last = None
                    for tc in range(4):
                        for ot in range(2):
                            p = tc * 2 + ot
                            last = e.matmul(
                                pg[p // 2][:, (p % 2) * 512:(p % 2 + 1) * 512],
                                guT[gs][:, tc * 128:(tc + 1) * 128],
                                wdT[gs][:, ot * 512:(ot + 1) * 512],
                                start=(hc == 0), stop=False)
                    return last
                P.emit("tensor", dmm,
                       waits=[ldg_ev[hc + 1], ldw_ev[hc + 1]]
                       + ([dpsum_free, c_tail] if hc == 0 else []),
                       incs=[(s_mm, 1)])
                guT_free[gs] = wv(s_mm)
                wdT_free[gs] = wv(s_mm)
            for hd in range(DC):
                def omm(e, hd=hd, half=half):
                    last = None
                    for tc in range(4):
                        for ot in range(2):
                            p = tc * 2 + ot
                            last = e.matmul(
                                pg[p // 2][:, (p % 2) * 512:(p % 2 + 1) * 512],
                                aot_sb[:, hd * 512 + tc * 128:
                                       hd * 512 + (tc + 1) * 128],
                                wo3[:, hd, half * 1024 + ot * 512:
                                    half * 1024 + (ot + 1) * 512],
                                start=False, stop=(hd == DC - 1))
                    return last
                P.emit("tensor", omm,
                       waits=[ao_ev_even[hd], ao_ev_odd[hd]] if half == 0 else [],
                       incs=[(s_mm, 1)])
            dmm_done = wv(s_mm)
            xre_ev = {}

            def emit_xload(tc, half=half, tt=tt):
                xs = tc % 2
                P.dma("gpsimd", xre[xs],
                      x_d[tt * 512 + tc * 128:tt * 512 + (tc + 1) * 128,
                          half * 1024:(half + 1) * 1024],
                      s_x, waits=[xre_free[xs]])
                xre_ev[tc] = wv(s_x)
            emit_xload(0)
            for tc in range(4):
                xs = tc % 2
                if tc + 1 < 4:
                    emit_xload(tc + 1)
                else:
                    P.dma("gpsimd", jx[:, 0:4], x_d[0:128, 0:4], s_x)
                    xre_ev[4] = wv(s_x)
                xw2 = xre_ev[tc + 1]
                ti = tt * 4 + tc
                for ot in range(2):
                    p = tc * 2 + ot
                    os_ = p % 4
                    def ev(e, p=p, ot=ot, xs=xs, ti=ti, os_=os_):
                        e.tensor_scalar(outst[os_], xre[xs][:, ot * 512:(ot + 1) * 512],
                                        stats_sb[:, 2 * ti:2 * ti + 1],
                                        stats_sb[:, 2 * ti + 1:2 * ti + 2],
                                        op0=ALU.mult, op1=ALU.add)
                        e.tensor_add(
                            outst[os_], outst[os_],
                            pg[p // 2][:, (p % 2) * 512:(p % 2 + 1) * 512])
                        return flushv(e)
                    P.emit("vector", ev, waits=[dmm_done, xw2, out_free[os_]],
                           incs=[(s_ve, 1)])
                    out_free[os_] = P.dma(
                        "sync",
                        out_d[tt * 512 + tc * 128:tt * 512 + (tc + 1) * 128,
                              half * 1024 + ot * 512:half * 1024 + (ot + 1) * 512],
                        outst[os_], s_st, waits=[wv(s_ve)])
                xre_free[xs] = wv(s_ve)
            dpsum_free = wv(s_ve)
        bar()

    P.emit("sync", lambda e: e.nop(), waits=[wv(s_st)])

    with nc.Block() as block:
        def mk(name):
            def f(eng):
                for (e, fn, waits, incs) in P.ops:
                    if e != name:
                        continue
                    for (h, v) in waits:
                        eng.wait_ge(h, v)
                    inst = fn(eng)
                    for (h, amt) in incs:
                        inst = inst.then_inc(h, amt)
            return f
        block.tensor(mk("tensor"))
        block.vector(mk("vector"))
        block.scalar(mk("scalar"))
        block.sync(mk("sync"))
        block.gpsimd(mk("gpsimd"))
    ES.close()
    return nc


_NC_CACHE = None


def kernel(L0_attn_x, L0_attn_norm_scale, L0_attn_split_scale, L0_attn_Wq,
           L0_attn_Wk, L0_attn_Wv, L0_attn_Wo, L0_ffn_wq_gate, L0_ffn_s_gate,
           L0_ffn_wq_up, L0_ffn_s_up, L0_ffn_wq_down, L0_ffn_s_down):
    global _NC_CACHE
    x = np.asarray(L0_attn_x, np.float32)
    wqT = np.ascontiguousarray(np.asarray(L0_attn_Wq, np.float32).T)
    wkT = np.ascontiguousarray(np.asarray(L0_attn_Wk, np.float32).T)
    wvT = np.ascontiguousarray(np.asarray(L0_attn_Wv, np.float32).T)
    woT = np.ascontiguousarray(np.asarray(L0_attn_Wo, np.float32).T)
    gT = np.ascontiguousarray(np.asarray(L0_ffn_wq_gate, np.int32).T)
    uT = np.ascontiguousarray(np.asarray(L0_ffn_wq_up, np.int32).T)
    wd = np.ascontiguousarray(np.asarray(L0_ffn_wq_down, np.int32).T)
    sg = np.ascontiguousarray(np.asarray(L0_ffn_s_gate, np.float32).reshape(1, HID))
    su = np.ascontiguousarray(np.asarray(L0_ffn_s_up, np.float32).reshape(1, HID))
    sdr = np.ascontiguousarray(
        np.asarray(L0_ffn_s_down, np.float32).reshape(1, D))
    nsr = np.ascontiguousarray(np.asarray(L0_attn_norm_scale, np.float32)
                               .reshape(DC, 128).T)
    ssr = np.ascontiguousarray(np.asarray(L0_attn_split_scale, np.float32)
                               .reshape(DC, 128).T)
    obd = np.zeros((128, 2), ml_dtypes.bfloat16)
    obd[0:64, 0] = 1
    obd[64:128, 1] = 1

    if _NC_CACHE is None:
        _NC_CACHE = build_nc()
    nc = _NC_CACHE

    in_maps = []
    for c in range(8):
        b, halfc = c // 2, c % 2
        xb = x[b]
        if halfc == 0:
            xc = xb
        else:
            xc = np.concatenate([xb[2048:], xb[:2048]], axis=0)
        in_maps.append(dict(
            x=np.ascontiguousarray(xc), wqT=wqT, wkT=wkT, wvT=wvT, woT=woT,
            gT=gT, uT=uT, wd=wd, sg=sg, su=su, sdr=sdr, nsr=nsr, ssr=ssr,
            ones_bd=obd))
    res = run_bass_kernel_spmd(nc, in_maps, core_ids=list(range(8)))
    out = np.empty((4, S, D), np.float32)
    for c in range(8):
        b, halfc = c // 2, c % 2
        out[b, halfc * 2048:(halfc + 1) * 2048] = res.results[c]["out"]
    return out
